# version 02
# baseline (speedup 1.0000x reference)
import numpy as np
import ml_dtypes
import concourse.bass as bass
import concourse.mybir as mybir

F32 = mybir.dt.float32
BF16 = mybir.dt.bfloat16
I8 = mybir.dt.int8
AF = mybir.ActivationFunctionType
ALU = mybir.AluOpType
AXX = mybir.AxisListType.X

NI0 = 44
NI1 = 64

EXP_SCALE = 1.0 / float(np.sqrt(128.0))
EXP_BIAS = -32.0 * EXP_SCALE
LN_EPS = 1e-5
L1_EPS = 1e-15

ENGS = ("sync", "gpsimd", "vector", "scalar", "tensor")


class Ctx:

    def __init__(self, nc):
        self.nc = nc
        self.p = {e: [] for e in ENGS}
        self.v = {}
        self.sem = {}

    def add(self, k, n=1):
        self.v[k] = self.v.get(k, 0) + n
        return self.v[k]

    def get(self, k):
        return self.v.get(k, 0)

    def nm(self, p):
        self.uid = getattr(self, "uid", 0) + 1
        return f"{p}_{self.uid}"

    def emit(self, eng, fn, waits=(), inc=None, inc_n=1):
        waits = tuple((self.sem[s], v) for (s, v) in waits)
        incs = (self.sem[inc], self.add(inc, inc_n)) if inc else None

        def step(e, fn=fn, waits=waits, incs=incs, inc_n=inc_n):
            for (sh, v) in waits:
                e.wait_ge(sh, v)
            ins = fn(e)
            if incs is not None:
                ins.then_inc(incs[0], inc_n)
        self.p[eng].append(step)
        return self.get(inc) if inc else None

    def wait(self, eng, waits):
        waits = tuple((self.sem[s], v) for (s, v) in waits)

        def step(e, waits=waits):
            for (sh, v) in waits:
                e.wait_ge(sh, v)
        self.p[eng].append(step)

    def dma(self, q, out, in_, waits=(), transpose=False, accum=False):
        eng = {"sq": "sync", "gq": "gpsimd", "aq": "scalar"}[q]
        if accum:
            fn = (lambda e, o=out, i=in_: e.dma_start(out=o, in_=i, accum_op=ALU.add))
        else:
            fn = (lambda e, o=out, i=in_, t=transpose: e.dma_start(out=o, in_=i, transpose=t))
        return self.emit(eng, fn, waits=waits, inc=q, inc_n=16)

    def mm(self, out, lhsT, rhs, start, stop, waits=(), inc=None):
        return self.emit("tensor", (lambda e, o=out, l=lhsT, r=rhs, st=start, sp=stop:
                                    e.matmul(o, l, r, start=st, stop=sp)),
                         waits=waits, inc=inc)

    def replay(self):
        with self.nc.allow_low_precision(reason="deliberate bf16 operands"), \
             self.nc.Block() as block:
            for name, fn in (("sync", block.sync), ("gpsimd", block.gpsimd),
                             ("vector", block.vector), ("scalar", block.scalar),
                             ("tensor", block.tensor)):
                steps = self.p[name]
                if steps:
                    fn(lambda e, steps=steps: [st(e) for st in steps])


class NS:
    pass


def alloc_common(nc, es, c):
    b = NS()
    b.big = es.enter_context(nc.sbuf_tensor("b_big", [128, 49152], BF16))
    b.sb = es.enter_context(nc.sbuf_tensor("s_buf", [128, 18432], BF16))
    b.xw0 = es.enter_context(nc.sbuf_tensor("xw0", [128, 2048], F32))
    b.xw1 = es.enter_context(nc.sbuf_tensor("xw1", [128, 2048], F32))
    b.xnt0 = es.enter_context(nc.sbuf_tensor("xnt0", [128, 2048], BF16))
    b.xnt1 = es.enter_context(nc.sbuf_tensor("xnt1", [128, 2048], BF16))
    b.st = es.enter_context(nc.sbuf_tensor("stt", [128, 48], F32))
    b.expt = es.enter_context(nc.sbuf_tensor("expt", [128, 1024], BF16))
    b.bcsb = es.enter_context(nc.sbuf_tensor("bcsb", [128, 512], BF16))
    b.lrec = es.enter_context(nc.sbuf_tensor("lrec", [1, 512], BF16))
    b.onesk = es.enter_context(nc.sbuf_tensor("onesk", [128, 1], BF16))
    b.ones1 = es.enter_context(nc.sbuf_tensor("ones1", [1, 128], BF16))
    b.cm = es.enter_context(nc.sbuf_tensor("cmm", [128, 256], BF16))
    b.scal = es.enter_context(nc.sbuf_tensor("scal", [128, 272], F32))
    b.vkbd = es.enter_context(nc.sbuf_tensor("vkbd", [128, 2112], BF16))
    b.tgf = es.enter_context(nc.sbuf_tensor("tgf", [128, 512], F32))
    b.tuf = es.enter_context(nc.sbuf_tensor("tuf", [128, 512], F32))
    b.acld = es.enter_context(nc.sbuf_tensor("acld", [128, 1024], F32))
    b.acout = es.enter_context(nc.sbuf_tensor("acout", [128, 1024], F32))
    b.ebias = es.enter_context(nc.sbuf_tensor("ebias", [128, 1], F32))
    c.b = b
    for sname in ("sq", "gq", "aq", "pe", "ve", "ac", "pr", "prm", "pv", "ccs"):
        c.sem[sname] = es.enter_context(nc.semaphore(sname))


SC_G, SC_U, SC_D = 0, 44, 88
SC_G1, SC_U1, SC_D1, SC_BP = 104, 168, 232, 248


def emit_init(c, dr):
    b = c.b
    c.dma("sq", b.cm[:], dr["cmask"].ap())
    c.dma("sq", b.scal[:, SC_G:SC_G + 44], dr["sc_g"].ap())
    c.dma("sq", b.scal[:, SC_U:SC_U + 44], dr["sc_u"].ap())
    c.dma("sq", b.scal[:, SC_D:SC_D + 16], dr["sc_d"].ap())
    c.dma("sq", b.scal[:, SC_G1:SC_G1 + 64], dr["sc_g1"].ap())
    c.dma("sq", b.scal[:, SC_U1:SC_U1 + 64], dr["sc_u1"].ap())
    c.dma("sq", b.scal[:, SC_D1:SC_D1 + 16], dr["sc_d1"].ap())
    c.dma("sq", b.scal[:, SC_BP:SC_BP + 16], dr["bprj"].ap())
    ld = c.get("sq")
    c.emit("vector", lambda e: e.memset(b.onesk[:], 1.0), inc="ve")
    c.emit("vector", lambda e: e.memset(b.ebias[:], EXP_BIAS), inc="ve")
    c.emit("vector", lambda e: e.memset(b.ones1[:], 1.0),
           waits=[("sq", ld)], inc="ve")
    c.init_ve = c.get("ve")


def ln_tile(c, xw, xnt, o, pre_waits_v, pre_waits_s):
    b = c.b
    sq2 = c.emit("scalar",
                 lambda e, xw=xw, o=o: e.activation(
                     b.big[:, 0:2048], xw[:], AF.Square,
                     accum_out=b.st[:, o + 1:o + 2]),
                 waits=pre_waits_s, inc="ac")

    s1v = c.emit("vector",
                 lambda e, o=o, xw=xw: e.reduce_sum(b.st[:, o:o + 1], xw[:], axis=AXX),
                 waits=list(pre_waits_v), inc="ve")
    mnv = c.emit("vector",
                 lambda e, o=o: e.tensor_scalar_mul(b.st[:, o + 2:o + 3],
                                                    b.st[:, o:o + 1], 1.0 / 2048.0),
                 waits=[("ve", s1v), ("ac", sq2)], inc="ve")

    def varq(e, o=o):
        e.tensor_scalar_mul(b.st[:, o + 3:o + 4], b.st[:, o + 1:o + 2], 1.0 / 2048.0)
        return e.tensor_mul(b.st[:, o + 4:o + 5], b.st[:, o + 2:o + 3],
                            b.st[:, o + 2:o + 3])
    vq = c.emit("vector", varq, waits=[("ve", mnv)], inc="ve")

    sbv = c.emit("vector",
                 lambda e, o=o: e.tensor_sub(b.st[:, o + 4:o + 5],
                                             b.st[:, o + 3:o + 4], b.st[:, o + 4:o + 5]),
                 waits=[("ve", vq)], inc="ve")
    st1 = c.emit("vector",
                 lambda e, o=o: e.tensor_scalar_add(b.st[:, o + 4:o + 5],
                                                    b.st[:, o + 4:o + 5], LN_EPS),
                 waits=[("ve", sbv)], inc="ve")
    sqr = c.emit("scalar",
                 lambda e, o=o: e.activation(b.st[:, o + 5:o + 6], b.st[:, o + 4:o + 5],
                                             AF.Sqrt),
                 waits=[("ve", st1)], inc="ac")
    rcv = c.emit("vector",
                 lambda e, o=o: e.reciprocal(b.st[:, o + 6:o + 7], b.st[:, o + 5:o + 6]),
                 waits=[("ac", sqr)], inc="ve")
    st2 = c.emit("vector",
                 lambda e, o=o: e.scalar_tensor_tensor(
                     b.st[:, o + 7:o + 8], b.st[:, o + 2:o + 3], -1.0,
                     b.st[:, o + 6:o + 7], op0=ALU.mult, op1=ALU.mult),
                 waits=[("ve", rcv)], inc="ve")
    return c.emit("scalar",
                  lambda e, xw=xw, xnt=xnt, o=o: e.activation(
                      xnt[:], xw[:], AF.Identity,
                      scale=b.st[:, o + 6:o + 7], bias=b.st[:, o + 7:o + 8]),
                  waits=[("ve", st2)], inc="ac")


def emit_ln_phase(c, src_ap, n_tiles, dsts):
    b = c.b
    norm_ac = {}
    store_sq = {}
    for g in range(n_tiles):
        xw = b.xw0 if g % 2 == 0 else b.xw1
        xnt = b.xnt0 if g % 2 == 0 else b.xnt1
        o = 16 * (g % 2)
        w = [("ac", norm_ac[g - 2])] if g - 2 in norm_ac else []
        ldg = c.dma("sq", xw[:], src_ap[g * 128:(g + 1) * 128, :], waits=w)
        pw_s = [("sq", ldg)]
        if g - 2 in store_sq:
            pw_s = pw_s + [("sq", store_sq[g - 2])]
        norm_ac[g] = ln_tile(c, xw, xnt, o, [("sq", ldg)], pw_s)
        for (dh, row0) in dsts:
            store_sq[g] = c.dma("sq", dh.ap()[row0 + g * 128: row0 + (g + 1) * 128, :],
                                xnt[:], waits=[("ac", norm_ac[g])])


BIG_XNF = 0
BIG_W = 32768


def load_strips_T(c, src_dram, row0, ncols, big0=BIG_XNF, waits=(), n_strips=16):
    b = c.b
    for ff in range(n_strips):
        w = waits if ff == 0 else ()
        c.dma("aq", b.big[:, big0 + ff * ncols: big0 + (ff + 1) * ncols],
              src_dram.ap()[row0:row0 + ncols, ff * 128:(ff + 1) * 128],
              waits=w, transpose=True)
    return c.get("aq")


def emit_proj(c, w_dram, xnt_dram, out_dram, tok_total, tmajor, stage0):
    b = c.b
    n_th = tok_total // 2048
    with c.nc.psum_tensor(c.nm("pj0"), [128, 512], F32) as p0, \
         c.nc.psum_tensor(c.nm("pj1"), [128, 512], F32) as p1:
        ps = [p0, p1]
        evac = {}
        stor = {}
        for wh in range(2):
            wld = None
            for dm in range(16):
                w = [("pe", c.get("pe"))] if dm == 0 else []
                wld = c.dma("gq", b.big[:, BIG_W + dm * 1024: BIG_W + (dm + 1) * 1024],
                            w_dram.ap()[dm * 128:(dm + 1) * 128, wh * 1024:(wh + 1) * 1024],
                            waits=w)
            for th in range(n_th):
                xld = load_strips_T(c, xnt_dram, th * 2048, 2048,
                                    waits=[("pe", c.get("pe")), ("sq", c.get("sq")),
                                           ("ac", c.get("ac"))])
                nblk = 8 if not tmajor else 16
                nsub = 4 if not tmajor else 2
                stg_w = 512 * nsub
                for blk in range(nblk):
                    stg = b.sb[:, stage0 + (blk % 2) * 2048: stage0 + (blk % 2) * 2048 + stg_w]
                    for sub in range(nsub):
                        k = blk * nsub + sub
                        pband = ps[k % 2]
                        for dm in range(16):
                            w = []
                            if dm == 0:
                                w = [("gq", wld), ("aq", xld)]
                                if (wh, th, k - 2) in evac:
                                    w.append(("ac", evac[(wh, th, k - 2)]))
                            if not tmajor:
                                lhsT = b.big[:, BIG_W + dm * 1024 + blk * 128:
                                             BIG_W + dm * 1024 + blk * 128 + 128]
                                rhs = b.big[:, BIG_XNF + dm * 2048 + sub * 512:
                                            BIG_XNF + dm * 2048 + sub * 512 + 512]
                            else:
                                lhsT = b.big[:, BIG_XNF + dm * 2048 + blk * 128:
                                             BIG_XNF + dm * 2048 + blk * 128 + 128]
                                rhs = b.big[:, BIG_W + dm * 1024 + sub * 512:
                                            BIG_W + dm * 1024 + sub * 512 + 512]
                            c.mm(pband[:, 0:512], lhsT, rhs,
                                 start=(dm == 0), stop=(dm == 15), waits=w,
                                 inc=("pe" if dm == 15 else None))
                        w = [("pe", c.get("pe"))]
                        if (wh, th, blk - 2) in stor:
                            w.append(("sq", stor[(wh, th, blk - 2)]))
                        evac[(wh, th, k)] = c.emit(
                            "scalar",
                            lambda e, stg=stg, sub=sub, pband=pband: e.activation(
                                stg[:, sub * 512:(sub + 1) * 512], pband[:, 0:512],
                                AF.Identity),
                            waits=w, inc="ac")
                    if not tmajor:
                        dst = out_dram.ap()[wh * 1024 + blk * 128: wh * 1024 + blk * 128 + 128,
                                            th * 2048:(th + 1) * 2048]
                    else:
                        dst = out_dram.ap()[th * 2048 + blk * 128: th * 2048 + blk * 128 + 128,
                                            wh * 1024:(wh + 1) * 1024]
                    stor[(wh, th, blk)] = c.dma(
                        "sq", dst, stg[:],
                        waits=[("ac", evac[(wh, th, blk * nsub + nsub - 1)])])


P3_Q = 32768
P3_K = 36864


def emit_attn0(c, d_kf, d_vt, d_qf):
    b = c.b
    with c.nc.psum_tensor(c.nm("pctx"), [128, 512], F32) as pctx, \
         c.nc.psum_tensor(c.nm("psc0"), [128, 512], F32) as psc0, \
         c.nc.psum_tensor(c.nm("psc1"), [128, 512], F32) as psc1, \
         c.nc.psum_tensor(c.nm("plp"), [1, 512], F32) as plp, \
         c.nc.psum_tensor(c.nm("pbc"), [128, 512], F32) as pbc:
        psc = [psc0, psc1]
        pe_head_end = {}
        for h in range(16):
            w = ([("pe", pe_head_end[h - 2])] if h - 2 in pe_head_end
                 else [("pe", c.get("pe")), ("ac", c.get("ac")), ("sq", c.get("sq"))])
            c.dma("sq", b.big[:, P3_Q + (h % 2) * 2048: P3_Q + (h % 2) * 2048 + 2048],
                  d_qf.ap()[h * 128:(h + 1) * 128, :], waits=w)
            c.dma("sq", b.big[:, P3_K + (h % 2) * 4096: P3_K + (h % 2) * 4096 + 4096],
                  d_kf.ap()[h * 128:(h + 1) * 128, :])
            for grp in range(4):
                c.dma("sq", bass.AP(b.sb, (h % 2) * 4096 + grp * 1024,
                                    [[18432, 128], [128, 8], [1, 128]]),
                      bass.AP(d_vt, h * 128 + grp * 8 * 128 * 2048,
                              [[2048, 128], [128 * 2048, 8], [1, 128]]))
            load_sq = c.get("sq")
            Q0 = P3_Q + (h % 2) * 2048
            K0 = P3_K + (h % 2) * 4096
            V0 = (h % 2) * 4096

            for qc in range(4):
                nk = 8 * qc + 8
                geo = []
                for kt in range(nk):
                    dstart = max(4 * qc, kt // 2)
                    off = (dstart - 4 * qc) * 128
                    geo.append((kt, off, 512 - off, kt // 2 >= 4 * qc))
                pend = {}
                pv_cnt = {}

                def emit_qk(j):
                    kt, off, N, masked = geo[j]
                    w = []
                    if j == 0:
                        w = [("sq", load_sq), ("ve", c.get("ve"))]
                    if j >= 2:
                        w.append(pend[j - 2])
                    c.mm(psc[j % 2][:, 0:N],
                         b.big[:, K0 + kt * 128: K0 + kt * 128 + 128],
                         b.big[:, Q0 + qc * 512 + off: Q0 + (qc + 1) * 512],
                         start=True, stop=True, waits=w, inc="pe")
                    sc_after = c.get("pe")
                    w = [("pe", sc_after)]
                    if j >= 2:
                        w.append(("pv", pv_cnt[j - 2]))
                    ex = c.emit("scalar",
                                lambda e, j=j, N=N: e.activation(
                                    b.expt[:, (j % 2) * 512: (j % 2) * 512 + N],
                                    psc[j % 2][:, 0:N], AF.Exp,
                                    scale=EXP_SCALE, bias=b.ebias[:, 0:1]),
                                waits=w, inc="pr")
                    if masked:
                        pend[j] = ("prm", c.emit(
                            "vector",
                            lambda e, j=j, kt=kt: e.tensor_mul(
                                b.expt[:, (j % 2) * 512: (j % 2) * 512 + 128],
                                b.expt[:, (j % 2) * 512: (j % 2) * 512 + 128],
                                b.cm[:, (kt % 2) * 128: (kt % 2) * 128 + 128]),
                            waits=[("pr", ex)], inc="prm"))
                    else:
                        pend[j] = ("pr", ex)

                def emit_pv(j):
                    kt, off, N, masked = geo[j]
                    c.mm(pctx[:, off:512],
                         b.sb[:, V0 + kt * 128: V0 + kt * 128 + 128],
                         b.expt[:, (j % 2) * 512: (j % 2) * 512 + N],
                         start=(j == 0), stop=(j == nk - 1), waits=[pend[j]])
                    pv_cnt[j] = c.mm(plp[0:1, off:512], b.onesk[:],
                                     b.expt[:, (j % 2) * 512: (j % 2) * 512 + N],
                                     start=(j == 0), stop=(j == nk - 1), inc="pv")

                emit_qk(0)
                for j in range(1, nk):
                    emit_qk(j)
                    emit_pv(j - 1)
                emit_pv(nk - 1)

                rc = c.emit("vector",
                            lambda e: e.reciprocal(b.lrec[0:1, 0:512], plp[0:1, 0:512]),
                            waits=[("pv", c.get("pv"))], inc="ve")
                bc = c.mm(pbc[:, 0:512], b.ones1[:], b.lrec[0:1, 0:512],
                          start=True, stop=True, waits=[("ve", rc)], inc="pe")

                cpv = c.emit("vector",
                             lambda e: e.tensor_copy(b.bcsb[:], pbc[:, 0:512]),
                             waits=[("pe", bc)], inc="ve")
                c.emit("vector",
                       lambda e, h=h, qc=qc: e.tensor_mul(
                           b.big[:, h * 2048 + qc * 512: h * 2048 + qc * 512 + 512],
                           pctx[:, 0:512], b.bcsb[:]),
                       waits=[("ve", cpv)], inc="ve")
            pe_head_end[h] = c.get("pe")
        c.attn0_ve = c.get("ve")


S_WO = 8192
S_AC = 12288


def emit_oproj(c, w_blocked, d_acc, ctx_wait, bias_col=None):
    b = c.b
    with c.nc.psum_tensor(c.nm("py0"), [128, 512], F32) as p0, \
         c.nc.psum_tensor(c.nm("py1"), [128, 512], F32) as p1:
        ps = [p0, p1]
        evac = {}
        stor = {}
        for o in range(16):
            w = [("pe", c.get("pe"))] if o >= 2 else [("pe", c.get("pe"))]
            wld = c.dma("gq", bass.AP(b.sb, S_WO + (o % 2) * 2048,
                                      [[18432, 128], [128, 16], [1, 128]]),
                        bass.AP(w_blocked, o * 2048 * 128,
                                [[128, 128], [128 * 128, 16], [1, 128]]),
                        waits=w)
            for qs in range(4):
                k = o * 4 + qs
                pband = ps[k % 2]
                for e_ in range(16):
                    w = []
                    if e_ == 0:
                        w = [("gq", wld), ctx_wait]
                        if k - 2 in evac:
                            w.append(("ac", evac[k - 2]))
                    c.mm(pband[:, 0:512],
                         b.sb[:, S_WO + (o % 2) * 2048 + e_ * 128:
                              S_WO + (o % 2) * 2048 + e_ * 128 + 128],
                         b.big[:, e_ * 2048 + qs * 512: e_ * 2048 + qs * 512 + 512],
                         start=(e_ == 0), stop=(e_ == 15), waits=w,
                         inc=("pe" if e_ == 15 else None))
                w = [("pe", c.get("pe")), ("ve", c.init_ve)]
                if k - 2 in stor:
                    w.append(("sq", stor[k - 2]))
                if bias_col is not None:
                    fn = (lambda e, k=k, pband=pband, o=o: e.activation(
                        b.acout[:, (k % 2) * 512: (k % 2) * 512 + 512],
                        pband[:, 0:512], AF.Identity,
                        bias=b.scal[:, bias_col + o: bias_col + o + 1]))
                else:
                    fn = (lambda e, k=k, pband=pband: e.activation(
                        b.acout[:, (k % 2) * 512: (k % 2) * 512 + 512],
                        pband[:, 0:512], AF.Identity))
                evac[k] = c.emit("scalar", fn, waits=w, inc="ac")
                stor[k] = c.dma("sq",
                                d_acc.ap()[o * 128:(o + 1) * 128, qs * 512:(qs + 1) * 512],
                                b.acout[:, (k % 2) * 512: (k % 2) * 512 + 512],
                                waits=[("ac", evac[k])])
    c.acc_sq = c.get("sq")


S_TG = 8192
S_A5 = 16384


def emit_ffn(c, gw_b, uw_b, dw_b, d_xnt_own, d_acc, d_accH, d_accL, ni, sc_gc, sc_uc, sc_dc):
    b = c.b
    XQ0 = ni * 512
    with c.nc.psum_tensor(c.nm("pg5"), [128, 512], F32) as pg, \
         c.nc.psum_tensor(c.nm("pu5"), [128, 512], F32) as pu, \
         c.nc.psum_tensor(c.nm("pd5"), [128, 512], F32) as pd0, \
         c.nc.psum_tensor(c.nm("pd5b"), [128, 512], F32) as pd1:
        pd = [pd0, pd1]
        for qcx in range(4):
            xld = None
            for ff in range(16):
                w = ([("pe", c.get("pe")), ("sq", c.get("sq")), ("ac", c.get("ac"))]
                     if ff == 0 else [])
                xld = c.dma("aq", b.big[:, XQ0 + ff * 512: XQ0 + (ff + 1) * 512],
                            d_xnt_own.ap()[qcx * 512:(qcx + 1) * 512, ff * 128:(ff + 1) * 128],
                            waits=w, transpose=True)
            gu_pe = {}
            for i in range(ni):
                w = ([("pe", gu_pe[i - 2])] if i - 2 in gu_pe
                     else [("pe", c.get("pe")), ("ve", c.get("ve"))])
                gld = c.dma("gq", bass.AP(b.sb, (i % 2) * 2048,
                                          [[18432, 128], [128, 16], [1, 128]]),
                            bass.AP(gw_b, i * 2048 * 128,
                                    [[128, 128], [128 * 128, 16], [1, 128]]),
                            waits=w)
                uld = c.dma("gq", bass.AP(b.sb, 4096 + (i % 2) * 2048,
                                          [[18432, 128], [128, 16], [1, 128]]),
                            bass.AP(uw_b, i * 2048 * 128,
                                    [[128, 128], [128 * 128, 16], [1, 128]]))
                for hh in range(16):
                    w = []
                    if hh == 0:
                        w = [("gq", uld), ("aq", xld), ("ac", c.get("ac"))]
                    c.mm(pg[:, 0:512],
                         b.sb[:, (i % 2) * 2048 + hh * 128: (i % 2) * 2048 + hh * 128 + 128],
                         b.big[:, XQ0 + hh * 512: XQ0 + (hh + 1) * 512],
                         start=(hh == 0), stop=(hh == 15), waits=w)
                for hh in range(16):
                    c.mm(pu[:, 0:512],
                         b.sb[:, 4096 + (i % 2) * 2048 + hh * 128:
                              4096 + (i % 2) * 2048 + hh * 128 + 128],
                         b.big[:, XQ0 + hh * 512: XQ0 + (hh + 1) * 512],
                         start=(hh == 0), stop=(hh == 15),
                         inc=("pe" if hh == 15 else None))
                gu_pe[i] = c.get("pe")
                ag = c.emit("scalar",
                            lambda e, i=i: e.activation(
                                b.tgf[:], pg[:, 0:512], AF.Silu,
                                scale=b.scal[:, sc_gc + i: sc_gc + i + 1]),
                            waits=[("pe", gu_pe[i]),
                                   ("ve", max(c.init_ve, c.get("ve")))],
                            inc="ac")
                au = c.emit("scalar",
                            lambda e, i=i: e.activation(
                                b.tuf[:], pu[:, 0:512], AF.Identity,
                                scale=b.scal[:, sc_uc + i: sc_uc + i + 1]),
                            inc="ac")
                c.emit("vector",
                       lambda e, i=i: e.tensor_mul(
                           b.big[:, i * 512: (i + 1) * 512],
                           b.tgf[:], b.tuf[:]),
                       waits=[("ac", au)], inc="ve")
            hdn_ve = c.get("ve")
            rmw_ve = {}
            for o in range(16):
                w = [("pe", c.get("pe"))]
                if o < 2:
                    w.append(("ve", hdn_ve))
                dld = c.dma("gq", bass.AP(b.sb, (o % 2) * (ni * 128),
                                          [[18432, 128], [128, ni], [1, 128]]),
                            bass.AP(dw_b, o * ni * 128 * 128,
                                    [[128, 128], [128 * 128, ni], [1, 128]]),
                            waits=w)
                wacc = [("sq", c.acc_sq)]
                if (o - 2) in rmw_ve:
                    wacc.append(("ve", rmw_ve[o - 2]))
                ald = c.dma("sq", b.acld[:, (o % 2) * 512: (o % 2) * 512 + 512],
                            d_acc.ap()[o * 128:(o + 1) * 128, qcx * 512:(qcx + 1) * 512],
                            waits=wacc)
                for i in range(ni):
                    w = []
                    if i == 0:
                        w = [("gq", dld), ("ve", hdn_ve)]
                        if o - 2 in rmw_ve:
                            w.append(("ve", rmw_ve[o - 2]))
                    c.mm(pd[o % 2][:, 0:512],
                         b.sb[:, (o % 2) * (ni * 128) + i * 128:
                              (o % 2) * (ni * 128) + i * 128 + 128],
                         b.big[:, i * 512:(i + 1) * 512],
                         start=(i == 0), stop=(i == ni - 1), waits=w,
                         inc=("pe" if i == ni - 1 else None))
                sv = c.emit(
                    "vector",
                    lambda e, o=o: e.scalar_tensor_tensor(
                        b.acout[:, (o % 2) * 512: (o % 2) * 512 + 512],
                        pd[o % 2][:, 0:512], b.scal[:, sc_dc + o: sc_dc + o + 1],
                        b.acld[:, (o % 2) * 512: (o % 2) * 512 + 512],
                        op0=ALU.mult, op1=ALU.add),
                    waits=[("pe", c.get("pe")), ("sq", ald), ("ve", c.init_ve)],
                    inc="ve")
                hv = c.emit(
                    "vector",
                    lambda e, o=o: e.tensor_copy(
                        b.sb[:, 16384 + (o % 2) * 512: 16384 + (o % 2) * 512 + 512],
                        b.acout[:, (o % 2) * 512: (o % 2) * 512 + 512]),
                    waits=[("ve", sv)], inc="ve")
                rmw_ve[o] = c.emit(
                    "vector",
                    lambda e, o=o: e.scalar_tensor_tensor(
                        b.sb[:, 17408 + (o % 2) * 512: 17408 + (o % 2) * 512 + 512],
                        b.sb[:, 16384 + (o % 2) * 512: 16384 + (o % 2) * 512 + 512],
                        -1.0,
                        b.acout[:, (o % 2) * 512: (o % 2) * 512 + 512],
                        op0=ALU.mult, op1=ALU.add),
                    waits=[("ve", hv)], inc="ve")
                c.dma("sq", d_accH.ap()[o * 128:(o + 1) * 128, qcx * 512:(qcx + 1) * 512],
                      b.sb[:, 16384 + (o % 2) * 512: 16384 + (o % 2) * 512 + 512],
                      waits=[("ve", rmw_ve[o])])
                c.dma("sq", d_accL.ap()[o * 128:(o + 1) * 128, qcx * 512:(qcx + 1) * 512],
                      b.sb[:, 17408 + (o % 2) * 512: 17408 + (o % 2) * 512 + 512],
                      waits=[("ve", rmw_ve[o])])
    c.acc_sq = c.get("sq")


def emit_residual(c, x_dram, accH, accL, out_dram, ln_dst):
    b = c.b
    last_sq = c.get("sq")
    for i in range(16):
        xl = c.dma("sq", b.xw0[:], x_dram.ap()[i * 128:(i + 1) * 128, :],
                   waits=[("sq", last_sq)])
        al = c.dma("aq", b.xnt0[:], accH.ap()[0:2048, i * 128:(i + 1) * 128],
                   waits=[("sq", c.acc_sq if i == 0 else last_sq),
                          ("ac", c.get("ac")), ("ve", c.get("ve"))],
                   transpose=True)
        al2 = c.dma("aq", b.big[:, 0:2048], accL.ap()[0:2048, i * 128:(i + 1) * 128],
                    transpose=True)
        r0 = c.emit("vector",
                    lambda e: e.tensor_add(b.xw1[:], b.xw0[:], b.xnt0[:]),
                    waits=[("sq", xl), ("aq", al2)], inc="ve")
        r = c.emit("vector",
                   lambda e: e.tensor_add(b.xw1[:], b.xw1[:], b.big[:, 0:2048]),
                   waits=[("ve", r0)], inc="ve")
        last_sq = c.dma("sq", out_dram.ap()[i * 128:(i + 1) * 128, :], b.xw1[:],
                        waits=[("ve", r)])
        if ln_dst is not None:
            nrm = ln_tile(c, b.xw1, b.xnt1, 0, [("ve", r)], [("ve", r)])
            last_sq = c.dma("sq", ln_dst.ap()[i * 128:(i + 1) * 128, :], b.xnt1[:],
                            waits=[("ac", nrm)])
    c.res_sq = c.get("sq")


def emit_qkv1(c, wq1_b, wkv1_b, d_xnt1, d_qf1, d_kt1, d_vt1):
    b = c.b
    xld = load_strips_T(c, d_xnt1, 0, 2048,
                        waits=[("pe", c.get("pe")), ("ve", c.get("ve")),
                               ("ac", c.get("ac")), ("sq", c.get("sq"))])
    with c.nc.psum_tensor(c.nm("pq7"), [128, 512], F32) as p0, \
         c.nc.psum_tensor(c.nm("pq7b"), [128, 512], F32) as p1:
        ps = [p0, p1]
        evac = {}
        stor = {}
        for j in range(16):
            w = [("pe", c.get("pe"))]
            wld = c.dma("gq", bass.AP(b.sb, S_WO + (j % 2) * 2048,
                                      [[18432, 128], [128, 16], [1, 128]]),
                        bass.AP(wq1_b, j * 2048 * 128,
                                [[128, 128], [128 * 128, 16], [1, 128]]),
                        waits=w)
            for ts in range(4):
                k = j * 4 + ts
                pband = ps[k % 2]
                for cch in range(16):
                    w = []
                    if cch == 0:
                        w = [("gq", wld), ("aq", xld)]
                        if k - 2 in evac:
                            w.append(("ac", evac[k - 2]))
                    c.mm(pband[:, 0:512],
                         b.sb[:, S_WO + (j % 2) * 2048 + cch * 128:
                              S_WO + (j % 2) * 2048 + cch * 128 + 128],
                         b.big[:, cch * 2048 + ts * 512: cch * 2048 + ts * 512 + 512],
                         start=(cch == 0), stop=(cch == 15), waits=w,
                         inc=("pe" if cch == 15 else None))
                w = [("pe", c.get("pe"))]
                if k - 2 in stor:
                    w.append(("sq", stor[k - 2]))
                evac[k] = c.emit("scalar",
                                 lambda e, k=k, pband=pband: e.activation(
                                     b.sb[:, S_AC + (k % 2) * 512: S_AC + (k % 2) * 512 + 512],
                                     pband[:, 0:512], AF.Relu),
                                 waits=w, inc="ac")
                stor[k] = c.dma("sq",
                                d_qf1.ap()[j * 128:(j + 1) * 128, ts * 512:(ts + 1) * 512],
                                b.sb[:, S_AC + (k % 2) * 512: S_AC + (k % 2) * 512 + 512],
                                waits=[("ac", evac[k])])
        evac = {}
        stor = {}
        for dd in range(8):
            is_k = dd < 4
            w = [("pe", c.get("pe")), ("ve", c.get("ve"))]
            wld = c.dma("gq", bass.AP(b.sb, (dd % 2) * 8192,
                                      [[18432, 128], [512, 16], [1, 512]]),
                        bass.AP(wkv1_b, dd * 2048 * 512,
                                [[512, 128], [512 * 128, 16], [1, 512]]),
                        waits=w)
            for tt in range(16):
                k = dd * 16 + tt
                pband = ps[k % 2]
                for cch in range(16):
                    w = []
                    if cch == 0:
                        w = [("gq", wld)]
                        if k - 2 in evac:
                            w.append(("ac", evac[k - 2]))
                    c.mm(pband[:, 0:512],
                         b.big[:, cch * 2048 + tt * 128: cch * 2048 + tt * 128 + 128],
                         b.sb[:, (dd % 2) * 8192 + cch * 512:
                              (dd % 2) * 8192 + cch * 512 + 512],
                         start=(cch == 0), stop=(cch == 15), waits=w,
                         inc=("pe" if cch == 15 else None))
                w = [("pe", c.get("pe"))]
                if k - 2 in stor:
                    w.append(("sq", stor[k - 2]))
                evac[k] = c.emit("scalar",
                                 lambda e, k=k, pband=pband, is_k=is_k: e.activation(
                                     b.sb[:, S_A5 + (k % 2) * 512:
                                          S_A5 + (k % 2) * 512 + 512],
                                     pband[:, 0:512],
                                     AF.Relu if is_k else AF.Identity),
                                 waits=w, inc="ac")
                dst = (d_kt1 if is_k else d_vt1)
                col0 = (dd % 4) * 512
                stor[k] = c.dma("sq",
                                dst.ap()[tt * 128:(tt + 1) * 128, col0:col0 + 512],
                                b.sb[:, S_A5 + (k % 2) * 512: S_A5 + (k % 2) * 512 + 512],
                                waits=[("ac", evac[k])])
    c.qkv1_sq = c.get("sq")


def emit_vk(c, d_kt1, d_vt1, d_vkbd):
    b = c.b
    z = c.emit("vector", lambda e: e.memset(b.vkbd[:], 0.0),
               waits=[("gq", c.get("gq")), ("ve", c.get("ve"))], inc="ve")
    zst = c.dma("gq", bass.AP(d_vkbd, 0, [[132, 128], [128 * 132, 16], [1, 132]]),
                bass.AP(b.vkbd, 0, [[2112, 128], [132, 16], [1, 132]]),
                waits=[("ve", z)])
    with c.nc.psum_tensor(c.nm("pvk"), [64, 2048], F32) as pvk:
        lds = {}
        for tt in range(16):
            w = [("pe", c.get("pe"))] if tt >= 2 else [("sq", c.qkv1_sq)]
            c.dma("sq", b.sb[:, (tt % 2) * 4096: (tt % 2) * 4096 + 2048],
                  d_kt1.ap()[tt * 128:(tt + 1) * 128, :], waits=w)
            lds[tt] = c.dma("sq", b.sb[:, (tt % 2) * 4096 + 2048: (tt % 2) * 4096 + 4096],
                            d_vt1.ap()[tt * 128:(tt + 1) * 128, :])
            for h in range(64):
                r0 = 32 * (h % 2)
                c0 = 64 * (h // 2)
                st = (tt == 0 and h % 16 < 2)
                w = ([("sq", lds[tt])] + ([("ve", c.init_ve)] if tt == 0 else [])
                     if h == 0 else [])
                c.mm(pvk[r0:r0 + 32, c0:c0 + 32],
                     b.sb[:, (tt % 2) * 4096 + h * 32: (tt % 2) * 4096 + h * 32 + 32],
                     b.sb[:, (tt % 2) * 4096 + 2048 + h * 32:
                          (tt % 2) * 4096 + 2048 + h * 32 + 32],
                     start=st, stop=(tt == 15 and h == 63), waits=w)
                c.mm(pvk[r0:r0 + 32, c0 + 32:c0 + 33],
                     b.sb[:, (tt % 2) * 4096 + h * 32: (tt % 2) * 4096 + h * 32 + 32],
                     b.onesk[:],
                     start=False, stop=False,
                     inc=("pe" if h == 63 else None))
        pall = c.get("pe")
        ev = c.emit("vector", lambda e: e.tensor_copy(b.big[0:64, 4096:6144], pvk[:]),
                    waits=[("pe", pall), ("gq", zst)], inc="ve")
        for h in range(64):
            g, a = h // 4, h % 4
            c.dma("gq",
                  bass.AP(d_vkbd, (g * 128 + 32 * a) * 132 + 33 * a, [[132, 32], [1, 33]]),
                  bass.AP(b.big, 4096 + 32 * (h % 2) * 49152 + 64 * (h // 2),
                          [[49152, 32], [1, 33]]),
                  waits=([("ve", ev)] if h == 0 else []))
    c.vk_gq = c.get("gq")


def emit_vk_sum_parts(c, d_vkbd, d_vkpa, d_vkpb):
    b = c.b
    bd = [[132, 128], [128 * 132, 16], [1, 132]]
    c.dma("gq", b.big[:, 0:2112], bass.AP(d_vkpa, 0, bd))
    l2 = c.dma("gq", b.big[:, 2112:4224], bass.AP(d_vkpb, 0, bd))
    c.emit("vector",
           lambda e: e.tensor_add(b.vkbd[:], b.big[:, 0:2112], b.big[:, 2112:4224]),
           waits=[("gq", l2)], inc="ve")
    c.vkbd_ve = c.get("ve")


def emit_vk_allreduce(c, d_vkbd, cc_out):
    b = c.b
    cc = c.emit("gpsimd",
                lambda e: e.collective_compute(
                    "AllReduce", ALU.add,
                    replica_groups=[[0, 1], [2, 3], [4, 5], [6, 7]],
                    ins=[d_vkbd.ap().opt()], outs=[cc_out.ap().opt()]),
                waits=[("gq", c.vk_gq)], inc="ccs")
    ld = c.dma("gq", b.vkbd[:],
               bass.AP(cc_out, 0, [[132, 128], [128 * 132, 16], [1, 132]]),
               waits=[("ccs", cc)])
    c.emit("vector", lambda e: e.engine_nop(), waits=[("gq", ld)], inc="ve")
    c.vkbd_ve = c.get("ve")


S_O1 = 4224


def emit_attn1(c, d_qf1, d_ctx1):
    b = c.b
    qld = None
    for s in range(16):
        w = [("pe", c.get("pe")), ("ve", c.get("ve")), ("sq", c.get("sq"))] if s == 0 else []
        qld = c.dma("sq", b.big[:, s * 2048:(s + 1) * 2048],
                    d_qf1.ap()[s * 128:(s + 1) * 128, :], waits=w)
    with c.nc.psum_tensor(c.nm("po9"), [128, 2048], F32) as po0, \
         c.nc.psum_tensor(c.nm("po9b"), [128, 2048], F32) as po1:
        po = [po0, po1]
        nrm_ve = {}
        stor = {}
        for tt in range(16):
            for q4 in range(4):
                k = tt * 4 + q4
                pband = po[k % 2]
                for a in range(4):
                    g = 4 * q4 + a
                    w = []
                    if a == 0:
                        w = [("ve", c.vkbd_ve), ("sq", qld)]
                        if k - 2 in nrm_ve:
                            w.append(nrm_ve[k - 2])
                    c.mm(pband[:, a * 512:a * 512 + 132],
                         b.big[:, g * 2048 + tt * 128: g * 2048 + tt * 128 + 128],
                         b.vkbd[:, g * 132:(g + 1) * 132],
                         start=True, stop=True, waits=w,
                         inc=("pe" if a == 3 else None))

                dn0 = c.emit("vector",
                             lambda e, pband=pband: e.tensor_scalar_add(
                                 b.st[:, 16:32],
                                 bass.AP(pband, 32, [[2048, 128], [512, 4], [33, 4]]),
                                 L1_EPS),
                             waits=[("pe", c.get("pe"))], inc="ve")
                dnv = c.emit("vector",
                             lambda e: e.reciprocal(b.st[:, 32:48], b.st[:, 16:32]),
                             waits=[("ve", dn0)], inc="ve")
                o1 = S_O1 + (tt % 2) * 2048
                last = None
                for j in range(16):
                    wj = [("ve", dnv)] if j == 0 else []
                    if j == 0 and q4 == 0 and tt - 2 in stor:
                        wj.append(("sq", stor[tt - 2]))
                    last = c.emit("scalar",
                                  lambda e, j=j, pband=pband, o1=o1, q4=q4: e.activation(
                                      b.sb[:, o1 + q4 * 512 + j * 32:
                                           o1 + q4 * 512 + j * 32 + 32],
                                      pband[:, 512 * (j // 4) + 33 * (j % 4):
                                            512 * (j // 4) + 33 * (j % 4) + 32],
                                      AF.Identity,
                                      scale=b.st[:, 32 + j: 33 + j]),
                                  waits=wj, inc="ac")
                nrm_ve[k] = None
                nrm_ac = last
                nrm_ve[k] = ("ac", nrm_ac)
            w = [nrm_ve[tt * 4 + 3]]
            if tt - 2 in stor:
                w.append(("sq", stor[tt - 2]))
            stor[tt] = c.dma("sq", d_ctx1.ap()[tt * 128:(tt + 1) * 128, :],
                             b.sb[:, S_O1 + (tt % 2) * 2048: S_O1 + (tt % 2) * 2048 + 2048],
                             waits=w)
    c.ctx1_sq = c.get("sq")


def finish(c):
    c.wait("sync", [("sq", c.get("sq")), ("gq", c.get("gq")), ("aq", c.get("aq"))])


def host_prep(inputs):
    bf = ml_dtypes.bfloat16
    f4 = np.float32
    sh = {}
    sh["wq"] = np.ascontiguousarray(np.asarray(inputs["L0_attn_q_wei"], f4))
    sh["wk"] = np.ascontiguousarray(np.asarray(inputs["L0_attn_k_wei"], f4))
    sh["wv"] = np.ascontiguousarray(np.asarray(inputs["L0_attn_v_wei"], f4))
    wo = np.asarray(inputs["L0_attn_out_wei"], f4)
    sh["wo_b"] = np.ascontiguousarray(wo.reshape(2048, 16, 128).transpose(1, 0, 2))
    gw = np.asarray(inputs["L0_ffn_gate_w"], f4)
    uw = np.asarray(inputs["L0_ffn_up_w"], f4)
    dw = np.asarray(inputs["L0_ffn_down_w"], f4)
    sh["gw_b"] = np.ascontiguousarray(gw.reshape(44, 128, 2048).transpose(0, 2, 1))
    sh["uw_b"] = np.ascontiguousarray(uw.reshape(44, 128, 2048).transpose(0, 2, 1))
    sh["dw_b"] = np.ascontiguousarray(dw.reshape(16, 128, 44, 128).transpose(0, 2, 3, 1))
    sh["sc_g"] = np.ascontiguousarray(np.asarray(inputs["L0_ffn_gate_s"], f4).reshape(44, 128).T)
    sh["sc_u"] = np.ascontiguousarray(np.asarray(inputs["L0_ffn_up_s"], f4).reshape(44, 128).T)
    sh["sc_d"] = np.ascontiguousarray(np.asarray(inputs["L0_ffn_down_s"], f4).reshape(16, 128).T)
    wqkv = np.asarray(inputs["L1_attn_w_qkv"], f4)
    sh["wq1_b"] = np.ascontiguousarray(wqkv[0:2048].reshape(16, 128, 2048).transpose(0, 2, 1))
    sh["wkv1_b"] = np.ascontiguousarray(wqkv[2048:6144].reshape(8, 512, 2048).transpose(0, 2, 1))
    wp = np.asarray(inputs["L1_attn_w_proj"], f4)
    sh["wp1_b"] = np.ascontiguousarray(wp.reshape(16, 128, 2048).transpose(0, 2, 1))
    sh["bprj"] = np.ascontiguousarray(np.asarray(inputs["L1_attn_b_proj"], f4).reshape(16, 128).T)
    wg1 = np.asarray(inputs["L1_ffn_w_gate"])
    wu1 = np.asarray(inputs["L1_ffn_w_up"])
    wd1 = np.asarray(inputs["L1_ffn_w_down"])
    sh["wg1_b"] = np.ascontiguousarray(wg1.reshape(2048, 64, 128).transpose(1, 0, 2))
    sh["wu1_b"] = np.ascontiguousarray(wu1.reshape(2048, 64, 128).transpose(1, 0, 2))
    sh["wd1_b"] = np.ascontiguousarray(wd1.reshape(64, 128, 16, 128).transpose(2, 0, 1, 3))
    sh["sc_g1"] = np.ascontiguousarray(np.asarray(inputs["L1_ffn_s_gate"], f4).reshape(64, 128).T)
    sh["sc_u1"] = np.ascontiguousarray(np.asarray(inputs["L1_ffn_s_up"], f4).reshape(64, 128).T)
    sh["sc_d1"] = np.ascontiguousarray(np.asarray(inputs["L1_ffn_s_down"], f4).reshape(16, 128).T)

    x = np.asarray(inputs["L0_attn_hidden_states"], f4)
    tri = np.triu(np.ones((128, 128), dtype=np.float32))
    cmask_p0 = np.concatenate([tri, np.zeros((128, 128), np.float32)], axis=1)
    cmask_p1 = np.concatenate([np.ones((128, 128), np.float32), tri], axis=1)
    percore = []
    for cidx in range(8):
        bidx, p = cidx // 2, cidx % 2
        xb = x[bidx]
        x_own = np.ascontiguousarray(xb.reshape(32, 128, 2048)[p::2].reshape(2048, 2048))
        cmv = (cmask_p0 if p == 0 else cmask_p1).astype(bf)
        percore.append({"x_full": np.ascontiguousarray(xb), "x_own": x_own,
                        "cmask": cmv})
    return sh, percore


def assemble_output(results, key="y_own"):
    out = np.zeros((4, 4096, 2048), dtype=np.float32)
    for cidx in range(8):
        bidx, p = cidx // 2, cidx % 2
        out[bidx].reshape(32, 128, 2048)[p::2] = results[cidx][key].reshape(16, 128, 2048)
    return out


SHARED_SPECS = {
    "wq": ([2048, 2048], F32), "wk": ([2048, 2048], F32), "wv": ([2048, 2048], F32),
    "wo_b": ([16, 2048, 128], F32),
    "gw_b": ([44, 2048, 128], F32), "uw_b": ([44, 2048, 128], F32),
    "dw_b": ([16, 44, 128, 128], F32),
    "sc_g": ([128, 44], F32), "sc_u": ([128, 44], F32), "sc_d": ([128, 16], F32),
    "wq1_b": ([16, 2048, 128], F32), "wkv1_b": ([8, 2048, 512], F32),
    "wp1_b": ([16, 2048, 128], F32), "bprj": ([128, 16], F32),
    "wg1_b": ([64, 2048, 128], I8), "wu1_b": ([64, 2048, 128], I8),
    "wd1_b": ([16, 64, 128, 128], I8),
    "sc_g1": ([128, 64], F32), "sc_u1": ([128, 64], F32), "sc_d1": ([128, 16], F32),
}
PERCORE_SPECS = {
    "x_full": ([4096, 2048], F32), "x_own": ([2048, 2048], F32),
    "cmask": ([128, 256], BF16),
}
SCRATCH_SPECS = {
    "d_xnt0": ([4096, 2048], BF16), "d_xnt0own": ([2048, 2048], BF16),
    "d_kf": ([2048, 4096], BF16), "d_vt": ([4096, 2048], BF16),
    "d_qf": ([2048, 2048], BF16),
    "d_acc0": ([2048, 2048], F32), "d_a0h": ([2048, 2048], BF16),
    "d_a0l": ([2048, 2048], BF16), "d_x1": ([2048, 2048], F32),
    "d_xnt1": ([2048, 2048], BF16),
    "d_qf1": ([2048, 2048], BF16), "d_kt1": ([2048, 2048], BF16),
    "d_vt1": ([2048, 2048], BF16),
    "d_vkbd": ([16, 128, 132], BF16),
    "vkpa": ([16, 128, 132], BF16), "vkpb": ([16, 128, 132], BF16),
    "cc_out": ([16, 128, 132], BF16),
    "d_ctx1": ([2048, 2048], BF16), "d_acc1": ([2048, 2048], F32),
    "d_a1h": ([2048, 2048], BF16), "d_a1l": ([2048, 2048], BF16),
    "y_own": ([2048, 2048], F32),
}
ALL_SPECS = {**SHARED_SPECS, **PERCORE_SPECS, **SCRATCH_SPECS}


def declare(nc, names, kind=None):
    out = {}
    for n in names:
        shp, dt = ALL_SPECS[n]
        if kind is None:
            out[n] = nc.dram_tensor(n, shp, dt)
        else:
            out[n] = nc.dram_tensor(n, shp, dt, kind=kind)
    return out


import contextlib

INIT_NAMES = ["cmask", "sc_g", "sc_u", "sc_d", "sc_g1", "sc_u1", "sc_d1", "bprj"]


def _mk(names_in, names_out, names_int):
    nc = bass.Bass("TRN2", target_bir_lowering=False)
    dr = {}
    dr.update(declare(nc, names_in, kind="ExternalInput"))
    dr.update(declare(nc, names_out, kind="ExternalOutput"))
    dr.update(declare(nc, names_int))
    return nc, dr


def emit_l0a(c, dr):
    emit_init(c, dr)
    emit_ln_phase(c, dr["x_full"].ap(), 32, [(dr["d_xnt0"], 0)])
    emit_ln_phase(c, dr["x_own"].ap(), 16, [(dr["d_xnt0own"], 0)])
    emit_proj(c, dr["wk"], dr["d_xnt0"], dr["d_kf"], 4096, False, 8192)
    emit_proj(c, dr["wv"], dr["d_xnt0"], dr["d_vt"], 4096, True, 8192)
    emit_proj(c, dr["wq"], dr["d_xnt0own"], dr["d_qf"], 2048, False, 8192)


def emit_l0b(c, dr):
    emit_init(c, dr)
    emit_attn0(c, dr["d_kf"], dr["d_vt"], dr["d_qf"])
    emit_oproj(c, dr["wo_b"], dr["d_acc0"], ("ve", c.attn0_ve))
    emit_ffn(c, dr["gw_b"], dr["uw_b"], dr["dw_b"], dr["d_xnt0own"], dr["d_acc0"],
             dr["d_a0h"], dr["d_a0l"], 44, SC_G, SC_U, SC_D)
    emit_residual(c, dr["x_own"], dr["d_a0h"], dr["d_a0l"], dr["d_x1"], dr["d_xnt1"])


def emit_l1a(c, dr):
    emit_init(c, dr)
    emit_qkv1(c, dr["wq1_b"], dr["wkv1_b"], dr["d_xnt1"], dr["d_qf1"],
              dr["d_kt1"], dr["d_vt1"])
    emit_vk(c, dr["d_kt1"], dr["d_vt1"], dr["d_vkbd"])


def emit_l1b(c, dr, fused):
    if not fused:
        emit_init(c, dr)
        emit_vk_sum_parts(c, dr["d_vkbd"], dr["vkpa"], dr["vkpb"])
    else:
        emit_vk_allreduce(c, dr["d_vkbd"], dr["cc_out"])
    emit_attn1(c, dr["d_qf1"], dr["d_ctx1"])
    ld = load_strips_T(c, dr["d_ctx1"], 0, 2048,
                       waits=[("pe", c.get("pe")), ("sq", c.ctx1_sq)])
    emit_oproj(c, dr["wp1_b"], dr["d_acc1"], ("aq", ld), bias_col=SC_BP)
    emit_ffn(c, dr["wg1_b"], dr["wu1_b"], dr["wd1_b"], dr["d_xnt1"], dr["d_acc1"],
             dr["d_a1h"], dr["d_a1l"], 64, SC_G1, SC_U1, SC_D1)
    emit_residual(c, dr["d_x1"], dr["d_a1h"], dr["d_a1l"], dr["y_own"], None)


def build(which, extra_out=()):
    table = {
        "l0a": (INIT_NAMES + ["x_full", "x_own", "wq", "wk", "wv"],
                ["d_xnt0own", "d_kf", "d_vt", "d_qf"], ["d_xnt0"]),
        "l0b": (INIT_NAMES + ["d_kf", "d_vt", "d_qf", "d_xnt0own", "x_own",
                              "wo_b", "gw_b", "uw_b", "dw_b"],
                ["d_x1", "d_xnt1"], ["d_acc0", "d_a0h", "d_a0l"]),
        "l1a": (INIT_NAMES + ["d_xnt1", "wq1_b", "wkv1_b"],
                ["d_qf1", "d_vkbd"], ["d_kt1", "d_vt1"]),
        "l1b": (INIT_NAMES + ["vkpa", "vkpb", "d_qf1", "d_x1", "d_xnt1",
                              "wp1_b", "wg1_b", "wu1_b", "wd1_b"],
                ["y_own"], ["d_vkbd", "d_ctx1", "d_acc1", "d_a1h", "d_a1l"]),
        "fused": (INIT_NAMES + ["x_full", "x_own", "wq", "wk", "wv", "wo_b",
                                "gw_b", "uw_b", "dw_b", "wq1_b", "wkv1_b",
                                "wp1_b", "wg1_b", "wu1_b", "wd1_b"],
                  ["y_own"],
                  ["d_xnt0", "d_xnt0own", "d_kf", "d_vt", "d_qf", "d_acc0",
                   "d_a0h", "d_a0l", "d_x1", "d_xnt1", "d_qf1", "d_kt1",
                   "d_vt1", "d_vkbd", "cc_out", "d_ctx1", "d_acc1",
                   "d_a1h", "d_a1l"]),
    }
    ni, no, nint = table[which]
    ni = [n for n in ni if n not in extra_out]
    nint = [n for n in nint if n not in extra_out]
    no = list(no) + list(extra_out)
    nc, dr = _mk(ni, no, nint)
    es = contextlib.ExitStack()
    c = Ctx(nc)
    with es:
        alloc_common(nc, es, c)
        if which == "l0a":
            emit_l0a(c, dr)
        elif which == "l0b":
            emit_l0b(c, dr)
        elif which == "l1a":
            emit_l1a(c, dr)
        elif which == "l1b":
            emit_l1b(c, dr, fused=False)
        elif which == "fused":
            emit_l0a(c, dr)
            emit_l0b(c, dr)
            emit_l1a(c, dr)
            emit_l1b(c, dr, fused=True)
        finish(c)
        c.replay()
    return nc


from concourse.bass_utils import run_bass_kernel_spmd

_NC_CACHE = {}


def _get_nc(which):
    if which not in _NC_CACHE:
        _NC_CACHE[which] = build(which)
    return _NC_CACHE[which]


def kernel(**inputs):
    sh, pc = host_prep(inputs)
    cores = list(range(8))

    im = []
    for cidx in range(8):
        m = {k: sh[k] for k in INIT_NAMES if k in sh}
        m.update({k: pc[cidx][k] for k in ("x_full", "x_own", "cmask")})
        m.update({k: sh[k] for k in ("wq", "wk", "wv")})
        im.append(m)
    r1 = run_bass_kernel_spmd(_get_nc("l0a"), im, core_ids=cores)

    im = []
    for cidx in range(8):
        m = {k: sh[k] for k in INIT_NAMES if k in sh}
        m["cmask"] = pc[cidx]["cmask"]
        m["x_own"] = pc[cidx]["x_own"]
        m.update({k: sh[k] for k in ("wo_b", "gw_b", "uw_b", "dw_b")})
        m.update({k: r1.results[cidx][k]
                  for k in ("d_kf", "d_vt", "d_qf", "d_xnt0own")})
        im.append(m)
    r2 = run_bass_kernel_spmd(_get_nc("l0b"), im, core_ids=cores)

    im = []
    for cidx in range(8):
        m = {k: sh[k] for k in INIT_NAMES if k in sh}
        m["cmask"] = pc[cidx]["cmask"]
        m.update({k: sh[k] for k in ("wq1_b", "wkv1_b")})
        m["d_xnt1"] = r2.results[cidx]["d_xnt1"]
        im.append(m)
    r3 = run_bass_kernel_spmd(_get_nc("l1a"), im, core_ids=cores)

    im = []
    for cidx in range(8):
        m = {k: sh[k] for k in INIT_NAMES if k in sh}
        m["cmask"] = pc[cidx]["cmask"]
        m.update({k: sh[k] for k in ("wp1_b", "wg1_b", "wu1_b", "wd1_b")})
        m["d_qf1"] = r3.results[cidx]["d_qf1"]
        m["d_x1"] = r2.results[cidx]["d_x1"]
        m["d_xnt1"] = r2.results[cidx]["d_xnt1"]
        m["vkpa"] = r3.results[cidx]["d_vkbd"]
        m["vkpb"] = r3.results[cidx ^ 1]["d_vkbd"]
        im.append(m)
    r4 = run_bass_kernel_spmd(_get_nc("l1b"), im, core_ids=cores)

    return assemble_output(r4.results)
